# version 18
# baseline (speedup 1.0000x reference)
import numpy as np
import concourse.bass as bass
import concourse.mybir as mybir
from concourse.bass_utils import run_bass_kernel_spmd

F32 = mybir.dt.float32
BF16 = mybir.dt.bfloat16
U8 = mybir.dt.uint8
AF = mybir.ActivationFunctionType
ALU = mybir.AluOpType
AX = mybir.AxisListType.X

B_TOT, C, D = 32, 2048, 1024
NCORE = 8
BPC = B_TOT // NCORE
NCC = C // 128
EPS_ATT = 1e-6
LN_EPS = 1e-5
THRESH = 0.7

ENGS = ("pe", "dve", "act", "pool", "sync")


class Tr:
    __slots__ = ("name", "w", "readers", "aliases")

    def __init__(self, name):
        self.name = name
        self.w = {}
        self.readers = {}
        self.aliases = []


class Sched:

    def __init__(self):
        self.counts = {k: 0 for k in ENGS}
        self.ops = {k: [] for k in ENGS}
        self.wm = {k: {} for k in ENGS}
        self.pending = {k: None for k in ENGS}

    def barrier(self):
        snap = {s: self.counts[s] for s in ENGS}
        for e in ENGS:
            self.pending[e] = dict(snap)

    def op(self, eng, fn, ins=(), outs=(), dma=None):
        waits = {}

        def add(s, v):
            if v <= 0 or v <= self.wm[eng].get(s, 0):
                return
            if waits.get(s, 0) < v:
                waits[s] = v

        if self.pending[eng] is not None:
            for s, v in self.pending[eng].items():
                if s != eng:
                    add(s, v)
            self.pending[eng] = None
        allouts = []
        for t in outs:
            allouts.append(t)
            allouts.extend(t.aliases)
        for t in ins:
            for s, v in t.w.items():
                add(s, v)
        for t in allouts:
            for s, v in t.w.items():
                add(s, v)
            for s, v in t.readers.items():
                add(s, v)
        sem = dma if dma else eng
        self.counts[sem] = self.counts.get(sem, 0) + (16 if dma else 1)
        val = self.counts[sem]
        wl = sorted(waits.items())
        for s, v in wl:
            self.wm[eng][s] = max(self.wm[eng].get(s, 0), v)
        self.ops[eng].append((wl, fn, sem, val))
        for t in outs:
            t.w[sem] = val
        for t in ins:
            t.readers[sem] = val
        return (sem, val)

    def raw(self, eng, fn):
        self.ops[eng].append(([], fn, None, None))


def build(n_batches=BPC):
    nc = bass.Bass("TRN2", target_bir_lowering=False)
    S = Sched()

    xdr = nc.dram_tensor("x", [n_batches, C, D], F32, kind="ExternalInput").ap()
    xtdr = nc.dram_tensor("xt", [n_batches, D, C], F32,
                          kind="ExternalInput").ap()
    odr = nc.dram_tensor("out", [n_batches, C, D], F32, kind="ExternalOutput").ap()
    wdr = {k: nc.dram_tensor(k, [D, D], F32, kind="ExternalInput").ap()
           for k in ("wq", "wk", "wv", "wf")}
    bdr = {k: nc.dram_tensor(k, [D], F32, kind="ExternalInput").ap()
           for k in ("bq", "bk", "bv", "bfn")}
    gdr = nc.dram_tensor("gamma", [1], F32, kind="ExternalInput").ap()
    adr = nc.dram_tensor("alpha", [1], F32, kind="ExternalInput").ap()
    wt_dram = nc.dram_tensor("wt_dram", [128, 8192], BF16, kind="Internal").ap()
    wtd_t = Tr("wt_dram")

    x_ch = xdr.rearrange("b (n p) d -> b n p d", p=128)
    o_ch = odr.rearrange("b (n p) d -> b n p d", p=128)
    xt4 = xtdr.rearrange("b (k p) c -> b k p c", p=128)
    w4 = {k: wdr[k].rearrange("(r p) d -> r p d", p=128)
          for k in ("wq", "wk", "wv", "wf")}

    ARENA = 212_480
    base_h = nc.alloc_sbuf_tensor("arena", [128, ARENA], U8)
    base_addr = nc.lookup_mloc(base_h).addr
    placed = []

    def place(name, shape, dtype, off):
        nbytes = int(np.prod(shape[1:])) * mybir.dt.size(dtype)
        assert off + nbytes <= ARENA, (name, off, nbytes)
        assert off % 32 == 0, name
        h = nc.virtual_sbuf_tensor(name, list(shape), dtype)
        nc._tensor(h.name, list(h.shape), h.dtype, type="SB")
        mloc = nc.lookup_mloc(h)
        mloc.allocated = True
        mloc.addr = base_addr + off
        t = Tr(name)
        placed.append((off, off + nbytes, t))
        return h, t

    WT_h, WT_t = {}, {}
    for i, k in enumerate(("wq", "wk", "wv")):
        WT_h[k], WT_t[k] = place("T_" + k, [128, 8192], BF16, i * 16_384)
    ident_h, ident_t = place("ident", [128, 128], BF16, 49_152)
    ones_h, ones_t = place("ones", [128, 512], BF16, 49_408)
    onesf_h, onesf_t = place("onesf", [128, 128], F32, 50_432)
    brow_h, brow_t = {}, {}
    for i, k in enumerate(("bq", "bk", "bv", "bfn")):
        brow_h[k], brow_t[k] = place("row_" + k, [1, 1024], BF16, 50_944 + i * 2_048)
    invqr_h, invqr_t = place("invq_row", [1, 1024], BF16, 59_136)
    invkr_h, invkr_t = place("invk_row", [1, 1024], BF16, 61_184)
    invqc_h, invqc_t = place("invq_col", [128, 8], F32, 63_232)
    gam_h, gam_t = place("gamma_col", [128, 1], F32, 63_264)
    alp_h, alp_t = place("alpha_col", [128, 1], F32, 63_296)
    sm_h, sm_t = place("smalls", [128, 64], F32, 63_328)
    mv_h, mv_t = place("mvecs", [128, 40], F32, 63_584)
    bvc_h, bvc_t = place("bv_col", [128, 8], F32, 63_776)
    eps_h, eps_t = place("eps_col", [128, 1], F32, 63_808)

    PS_h = nc.alloc_psum_tensor("PS", [128, 4096], F32)
    bank_tr = [Tr(f"psbank{i}") for i in range(8)]

    def psv(c0, c1):
        return PS_h[:, c0:c1]

    def pbt(c0, c1):
        return bank_tr[c0 // 512:(c1 + 511) // 512]

    xFq_h, xFq_t = zip(*[place(f"xFq{q}", [128, 4096], BF16, 65_024 + q * 8_192)
                         for q in range(4)])
    xq3 = [h[:, :].rearrange("p (k c) -> p k c", c=512) for h in xFq_h]
    M_h, M_t = place("M_all", [128, 8192], BF16, 65_024)
    wtsb_h, wtsb_t = place("wt_sb", [128, 8192], BF16, 81_408)
    wnatc_h, wnatc_t = zip(*[place(f"wnat{r}", [128, 1024], F32,
                                   97_792 + r * 4_096) for r in range(8)])
    QT_h, QT_t = place("Q_T", [128, 16384], BF16, 97_792)
    xr_h, xr_t = zip(*[place(f"xr{s}", [128, 1024], F32, 97_792 + s * 4_096)
                       for s in range(2)])
    ot_h, ot_t = zip(*[place(f"out_t{s}", [128, 1024], F32, 105_984 + s * 4_096)
                       for s in range(2)])
    res_h, res_t = zip(*[place(f"res{s}", [128, 1024], F32, 114_176 + s * 4_096)
                         for s in range(2)])
    res3_h, res3_t = place("res2_", [128, 1024], F32, 155_136)
    res_h = list(res_h) + [res3_h]
    res_t = list(res_t) + [res3_t]
    ybf_h, ybf_t = zip(*[place(f"ybf{s}", [128, 1024], BF16, 122_368 + s * 2_048)
                         for s in range(3)])
    wnbf_h, wnbf_t = place("wnbf", [128, 8192], BF16, 130_560)
    KT_h, KT_t = place("K_T", [128, 16384], BF16, 130_560)
    MT_h, MT_t = place("MT_all", [128, 8192], BF16, 130_560)
    yF_h, yF_t = zip(*[place(f"yF{s}", [128, 1024], BF16, 146_944 + s * 2_048)
                       for s in range(2)])
    sqln_h, sqln_t = place("sq_ln", [128, 1024], F32, 151_040)
    VF_h, VF_t = place("V_F", [128, 16384], BF16, 163_328)
    wfnc_h, wfnc_t = zip(*[place(f"wfn{r}", [128, 1024], F32,
                                 163_328 + r * 4_096) for r in range(8)])
    ternc_h, ternc_t = zip(*[place(f"tern{t}", [128, 1024], BF16,
                                   163_328 + t * 2_048) for t in range(8)])
    ternW_h, ternW_t = place("ternW", [128, 8192], BF16, 163_328)
    wnbfv_h, wnbfv_t = place("wnbf_v", [128, 8192], BF16, 179_712)
    wst_h, wst_t = place("wt_stage", [128, 8192], BF16, 179_712)
    tmpa_h, tmpa_t = place("tmpa", [128, 1024], F32, 196_096)
    tmpb_h, tmpb_t = place("tmpb", [128, 1024], F32, 200_192)
    s1_h, s1_t = zip(*[place(f"s1_{s}", [128, 1024], BF16, 196_096 + s * 2_048)
                       for s in range(2)])
    bqb_h, bqb_t = place("bq_bcast", [128, 1024], BF16, 204_288)
    bkb_h, bkb_t = place("bk_bcast", [128, 1024], BF16, 206_336)
    sqq_h, sqq_t = place("sq_q", [128, 1024], BF16, 208_384)
    sqk_h, sqk_t = place("sq_k", [128, 1024], BF16, 210_432)
    invkb_h, invkb_t = place("invk_bcast", [128, 1024], BF16, 210_432)

    for i, (lo, hi, t) in enumerate(placed):
        for j, (lo2, hi2, t2) in enumerate(placed):
            if i != j and lo < hi2 and lo2 < hi:
                t.aliases.append(t2)

    def dma(dst_ap, src_ap, ins, outs, sem, eng="sync"):
        S.op(eng, lambda e, d=dst_ap, s=src_ap: e.dma_start(out=d, in_=s),
             ins=list(ins), outs=list(outs), dma=sem)

    def transpose_group(psum_ap, blocks, ins, ps_trs):
        def fn(e, ps=psum_ap, bl=tuple(blocks)):
            last = None
            for out_off, lhs in bl:
                last = e.matmul(ps[:, out_off:out_off + 128], lhs,
                                ident_h[:, :], start=True, stop=True)
            return last
        S.op("pe", fn, ins=list(ins) + [ident_t], outs=list(ps_trs))

    EVAC = ("dve", "act")

    def evac_copy(eng, dst_ap, src_ap, ins, outs):
        if eng == "act":
            S.op("act", lambda e, d=dst_ap, s=src_ap: e.activation(
                d, s, AF.Copy), ins=list(ins), outs=list(outs))
        else:
            S.op("dve", lambda e, d=dst_ap, s=src_ap: e.tensor_copy(d, s),
                 ins=list(ins), outs=list(outs))

    S.op("pool", lambda e: e.memset(ones_h[:, :], 1.0), outs=[ones_t])
    S.op("pool", lambda e: e.memset(eps_h[:, :], LN_EPS), outs=[eps_t])
    S.op("pool", lambda e: e.memset(onesf_h[:, :], 1.0), outs=[onesf_t])
    S.op("pool", lambda e: e.iota(tmpa_h[:, 0:128], [[1, 128]],
                                  channel_multiplier=0,
                                  allow_small_or_imprecise_dtypes=True),
         outs=[tmpa_t])
    S.op("pool", lambda e: e.iota(tmpb_h[:, 0:128], [[0, 128]],
                                  channel_multiplier=1,
                                  allow_small_or_imprecise_dtypes=True),
         outs=[tmpb_t])
    S.op("dve", lambda e: e.tensor_tensor(ident_h[:, :], tmpa_h[:, 0:128],
                                          tmpb_h[:, 0:128], ALU.is_equal),
         ins=[tmpa_t, tmpb_t], outs=[ident_t])
    for k in ("bq", "bk", "bv", "bfn"):
        dma(brow_h[k][0:1, :], bdr[k][None, :], [], [brow_t[k]],
            "D_b" + k, eng="pool")
    for q in range(4):
        for k in range(8):
            dma(xq3[q][:, k, :], xt4[0, k][:, q * 512:(q + 1) * 512],
                [], [xFq_t[q]], f"D_xq{q}", eng="pool")
    g_sb = sm_h[0:1, 32:33]
    a_sb = sm_h[0:1, 33:34]
    dma(g_sb, gdr[None, :], [], [sm_t], "D_sm")
    dma(a_sb, adr[None, :], [], [sm_t], "D_sm")
    for r in range(8):
        dma(wfnc_h[r][:, :], w4["wf"][r], [], [wfnc_t[r]], f"D_wf{r}",
            eng="act")
    for r in range(8):
        S.op("act", lambda e, r=r: e.activation(
            tmpa_h[:, :], wfnc_h[r][:, :], AF.Abs,
            accum_out=sm_h[:, 48 + r:49 + r]),
             ins=[wfnc_t[r]], outs=[tmpa_t, sm_t])

    ps_b = psv(2048, 2560)
    ps_b2 = psv(2560, 3072)

    def bvcols(e):
        last = None
        for jj in range(8):
            last = e.matmul(ps_b[:, jj:jj + 1],
                            brow_h["bv"][0:1, jj * 128:(jj + 1) * 128],
                            ones_h[0:1, 0:1], start=True, stop=True)
        return last
    S.op("pe", bvcols, ins=[brow_t["bv"], ones_t], outs=pbt(2048, 2560))
    S.op("dve", lambda e: e.tensor_copy(bvc_h[:, :], ps_b[:, 0:8]),
         ins=pbt(2048, 2560), outs=[bvc_t])
    S.op("pe", lambda e: e.matmul(ps_b[:, 8:9], onesf_h[0:1, :], g_sb,
                                  start=True, stop=True),
         ins=[onesf_t, sm_t], outs=pbt(2048, 2560))
    S.op("pe", lambda e: e.matmul(ps_b[:, 9:10], onesf_h[0:1, :], a_sb,
                                  start=True, stop=True),
         ins=[onesf_t, sm_t], outs=pbt(2048, 2560))
    S.op("dve", lambda e: e.tensor_copy(gam_h[:, :], ps_b[:, 8:9]),
         ins=pbt(2048, 2560), outs=[gam_t])
    S.op("dve", lambda e: e.tensor_copy(alp_h[:, :], ps_b[:, 9:10]),
         ins=pbt(2048, 2560), outs=[alp_t])
    for bb_h, bb_t, key in ((bqb_h, bqb_t, "bq"), (bkb_h, bkb_t, "bk")):
        for h in range(2):
            S.op("pe", lambda e, h=h, key=key: e.matmul(
                ps_b2[:, 0:512], ones_h[0:1, 0:128],
                brow_h[key][0:1, h * 512:(h + 1) * 512],
                start=True, stop=True),
                 ins=[brow_t[key], ones_t], outs=pbt(2560, 3072))
            S.op("dve", lambda e, h=h, bb_h=bb_h: e.tensor_copy(
                bb_h[:, h * 512:(h + 1) * 512], ps_b2[:, 0:512]),
                 ins=pbt(2560, 3072), outs=[bb_t])

    def load_weight(key, dst_h, dst_t):
        for r in range(8):
            dma(wnatc_h[r][:, :], w4[key][r], [], [wnatc_t[r]], f"D_wn{r}")
        for r in range(8):
            S.op("dve", lambda e, r=r, dst_h=dst_h: e.tensor_copy(
                dst_h[:, r * 1024:(r + 1) * 1024], wnatc_h[r][:, :]),
                 ins=[wnatc_t[r]], outs=[dst_t])

    def transpose_weight(src_h, src_ts, out_h, out_t, s8):
        c0 = (s8 % 2) * 1024
        ps_w = psv(c0, c0 + 1024)
        blocks = [(r * 128, src_h[:, r * 1024 + s8 * 128:
                                  r * 1024 + s8 * 128 + 128])
                  for r in range(8)]
        transpose_group(ps_w, blocks, src_ts, pbt(c0, c0 + 1024))
        evac_copy(EVAC[s8 % 2], out_h[:, s8 * 1024:(s8 + 1) * 1024],
                  ps_w[:, :], pbt(c0, c0 + 1024), [out_t])

    load_weight("wq", wnbf_h, wnbf_t)
    for s8 in range(8):
        transpose_weight(wnbf_h, [wnbf_t], WT_h["wq"], WT_t["wq"], s8)
    S.op("dve", lambda e: e.tensor_reduce(sm_h[:, 34:35], sm_h[:, 48:56],
                                          AX, ALU.add),
         ins=[sm_t], outs=[sm_t])
    ps_thr = psv(3072, 3584)
    S.op("pe", lambda e: e.matmul(ps_thr[:, 0:1], onesf_h[:, :],
                                  sm_h[:, 34:35], start=True, stop=True),
         ins=[onesf_t, sm_t], outs=pbt(3072, 3584))
    S.op("dve", lambda e: e.tensor_scalar(sm_h[:, 35:36], ps_thr[:, 0:1],
                                          THRESH / float(D * D), None,
                                          ALU.mult),
         ins=pbt(3072, 3584), outs=[sm_t])
    S.op("dve", lambda e: e.tensor_scalar(sm_h[:, 36:37], sm_h[:, 35:36],
                                          -1.0, None, ALU.mult),
         ins=[sm_t], outs=[sm_t])
    load_weight("wk", wnbf_h, wnbf_t)
    for s8 in range(8):
        transpose_weight(wnbf_h, [wnbf_t], WT_h["wk"], WT_t["wk"], s8)
    for t in range(8):
        S.op("dve", lambda e, t=t: e.tensor_scalar(
            tmpa_h[:, :], wfnc_h[t][:, :], sm_h[:, 35:36], alp_h[:, 0:1],
            ALU.is_gt, ALU.mult),
             ins=[wfnc_t[t], sm_t, alp_t], outs=[tmpa_t])
        S.op("dve", lambda e, t=t: e.tensor_scalar(
            tmpb_h[:, :], wfnc_h[t][:, :], sm_h[:, 36:37], alp_h[:, 0:1],
            ALU.is_lt, ALU.mult),
             ins=[wfnc_t[t], sm_t, alp_t], outs=[tmpb_t])
        S.op("dve", lambda e, t=t: e.tensor_tensor(
            ternc_h[t][:, :], tmpa_h[:, :], tmpb_h[:, :], ALU.subtract),
             ins=[tmpa_t, tmpb_t], outs=[ternc_t[t]])
    load_weight("wv", wnbfv_h, wnbfv_t)

    for b in range(n_batches):
        ps_q, ps_k = psv(2048, 3072), psv(3072, 4096)
        ps_q_t, ps_k_t = pbt(2048, 3072), pbt(3072, 4096)

        def proj_group(ps_ap, ps_tr, wk_, i):
            def fn(e, ps=ps_ap, i=i, wk_=wk_):
                last = None
                for kk in range(8):
                    lhs = xq3[i // 4][:, kk, (i % 4) * 128:(i % 4) * 128 + 128]
                    for h in range(2):
                        last = e.matmul(
                            ps[:, h * 512:(h + 1) * 512], lhs,
                            WT_h[wk_][:, kk * 1024 + h * 512:
                                       kk * 1024 + (h + 1) * 512],
                            start=(kk == 0), stop=(kk == 7))
                return last
            S.op("pe", fn, ins=[xFq_t[i // 4], WT_t[wk_]], outs=list(ps_tr))

        def evac_qk(dst_ap, ps_ap, ps_tr, bb_h, bb_t, out_t):
            S.op("dve", lambda e, d=dst_ap, ps=ps_ap, bb=bb_h:
                 e.scalar_tensor_tensor(d, ps[:, :], 1.0, bb[:, :],
                                        ALU.mult, ALU.add),
                 ins=list(ps_tr) + [bb_t], outs=[out_t])

        extra_pe = {}
        if b == 0:
            for s8 in range(8):
                extra_pe.setdefault(3 + s8 // 2, []).append(
                    lambda s8=s8: transpose_weight(
                        wnbfv_h, [wnbfv_t], WT_h["wv"], WT_t["wv"], s8))
            for s8 in range(8):
                extra_pe.setdefault(7 + s8 // 2, []).append(
                    lambda s8=s8: transpose_weight(
                        ternW_h, list(ternc_t), wst_h, wst_t, s8))
            extra_pe.setdefault(11, []).append(
                lambda: dma(wt_dram, wst_h[:, :], [wst_t], [wtd_t], "D_wtd"))

        for i in range(NCC):
            proj_group(ps_q, ps_q_t, "wq", i)
            evac_qk(QT_h[:, i * 1024:(i + 1) * 1024], ps_q, ps_q_t,
                    bqb_h, bqb_t, QT_t)
            proj_group(ps_k, ps_k_t, "wk", i)
            evac_qk(KT_h[:, i * 1024:(i + 1) * 1024], ps_k, ps_k_t,
                    bkb_h, bkb_t, KT_t)
            for fn in extra_pe.get(i, []):
                fn()

        ps_v = (psv(0, 1024), psv(1024, 2048))
        ps_v_t = (pbt(0, 1024), pbt(1024, 2048))
        ps_cq, ps_ck = psv(2048, 3072), psv(3072, 4096)
        ps_cq_t, ps_ck_t = pbt(2048, 3072), pbt(3072, 4096)

        def colsum_group(ps_ap, ps_tr, sqh, sqtr, i):
            def fn(e, ps=ps_ap, sqh=sqh, i=i):
                last = None
                for h in range(2):
                    last = e.matmul(ps[0:1, h * 512:(h + 1) * 512],
                                    ones_h[:, 0:1],
                                    sqh[:, h * 512:(h + 1) * 512],
                                    start=(i == 0), stop=(i == NCC - 1))
                return last
            S.op("pe", fn, ins=[sqtr, ones_t], outs=list(ps_tr))

        for it in range(16):
            l, half = it // 2, it % 2
            s = it % 2

            def vfn(e, ps=ps_v[s], l=l, half=half):
                last = None
                for kk in range(8):
                    lhs = WT_h["wv"][:, kk * 1024 + l * 128:
                                     kk * 1024 + l * 128 + 128]
                    for q2 in range(2):
                        last = e.matmul(ps[:, q2 * 512:(q2 + 1) * 512],
                                        lhs, xq3[half * 2 + q2][:, kk, :],
                                        start=(kk == 0), stop=(kk == 7))
                return last
            S.op("pe", vfn, ins=[xFq_t[half * 2], xFq_t[half * 2 + 1],
                                 WT_t["wv"]], outs=list(ps_v_t[s]))
            dst = VF_h[:, l * 2048 + half * 1024:l * 2048 + half * 1024 + 1024]
            S.op("dve", lambda e, d=dst, ps=ps_v[s], l=l: e.tensor_scalar(
                d, ps[:, :], bvc_h[:, l:l + 1], None, ALU.add),
                 ins=list(ps_v_t[s]) + [bvc_t], outs=[VF_t])
            qsl = QT_h[:, it * 1024:(it + 1) * 1024]
            ksl = KT_h[:, it * 1024:(it + 1) * 1024]
            S.op("pool", lambda e, a=qsl: e.tensor_tensor(
                sqq_h[:, :], a, a, ALU.mult), ins=[QT_t], outs=[sqq_t])
            colsum_group(ps_cq, ps_cq_t, sqq_h, sqq_t, it)
            S.op("pool", lambda e, a=ksl: e.tensor_tensor(
                sqk_h[:, :], a, a, ALU.mult), ins=[KT_t], outs=[sqk_t])
            colsum_group(ps_ck, ps_ck_t, sqk_h, sqk_t, it)

        stage = s1_h[0]
        S.op("act", lambda e: e.activation(stage[0:1, 0:1024],
                                           ps_cq[0:1, 0:1024], AF.Ln),
             ins=list(ps_cq_t), outs=[s1_t[0]])
        S.op("act", lambda e: e.activation(invqr_h[0:1, :],
                                           stage[0:1, 0:1024], AF.Exp,
                                           scale=-0.5),
             ins=[s1_t[0]], outs=[invqr_t])
        S.op("act", lambda e: e.activation(stage[0:1, 0:1024],
                                           ps_ck[0:1, 0:1024], AF.Ln),
             ins=list(ps_ck_t), outs=[s1_t[0]])
        S.op("act", lambda e: e.activation(invkr_h[0:1, :],
                                           stage[0:1, 0:1024], AF.Exp,
                                           scale=-0.5),
             ins=[s1_t[0]], outs=[invkr_t])

        dma(wtsb_h[:, :], wt_dram, [wtd_t], [wtsb_t], "D_wtsb")

        ps_a = (psv(0, 1536), psv(1536, 3072))
        ps_a_t = (pbt(0, 1536), pbt(1536, 3072))
        for j in range(8):
            s = j % 2

            def afn(e, ps=ps_a[s], j=j):
                last = None
                for i in range(NCC):
                    lhs = QT_h[:, i * 1024 + j * 128:i * 1024 + j * 128 + 128]
                    st, sp = (i == 0), (i == NCC - 1)
                    last = e.matmul(ps[:, 0:512], lhs,
                                    KT_h[:, i * 1024:i * 1024 + 512],
                                    start=st, stop=sp)
                    last = e.matmul(ps[:, 512:1024], lhs,
                                    KT_h[:, i * 1024 + 512:i * 1024 + 1024],
                                    start=st, stop=sp)
                    last = e.matmul(ps[:, 1024:1025], lhs, ones_h[:, 0:1],
                                    start=st, stop=sp)
                return last
            S.op("pe", afn, ins=[QT_t, KT_t, ones_t], outs=list(ps_a_t[s]))
            if j == 0:
                def invq_cols(e):
                    last = None
                    for jj in range(8):
                        last = e.matmul(psv(3584, 3592)[:, jj:jj + 1],
                                        invqr_h[0:1, jj * 128:(jj + 1) * 128],
                                        ones_h[0:1, 0:1],
                                        start=True, stop=True)
                    return last
                S.op("pe", invq_cols, ins=[invqr_t, ones_t],
                     outs=pbt(3584, 3592))
                S.op("dve", lambda e: e.tensor_copy(invqc_h[:, :],
                                                    psv(3584, 3592)[:, 0:8]),
                     ins=pbt(3584, 3592), outs=[invqc_t])
                for h in range(2):
                    S.op("pe", lambda e, h=h: e.matmul(
                        psv(3072, 3584)[:, 0:512], ones_h[0:1, 0:128],
                        invkr_h[0:1, h * 512:(h + 1) * 512],
                        start=True, stop=True),
                         ins=[invkr_t, ones_t], outs=pbt(3072, 3584))
                    S.op("dve", lambda e, h=h: e.tensor_copy(
                        invkb_h[:, h * 512:(h + 1) * 512],
                        psv(3072, 3584)[:, 0:512]),
                         ins=pbt(3072, 3584), outs=[invkb_t])
            S.op("act", lambda e, s=s, j=j: e.activation(
                s1_h[s][:, :], ps_a[s][:, 0:1024], AF.Copy,
                bias=0.0, scale=invqc_h[:, j:j + 1]),
                 ins=list(ps_a_t[s]) + [invqc_t], outs=[s1_t[s]])
            msl = M_h[:, j * 1024:(j + 1) * 1024]
            S.op("dve", lambda e, s=s, msl=msl: e.tensor_tensor(
                msl, s1_h[s][:, :], invkb_h[:, :], ALU.mult),
                 ins=[s1_t[s], invkb_t], outs=[M_t])
            S.op("dve", lambda e, msl=msl, j=j: e.tensor_scalar(
                msl, msl, 1.0, 0.0, ALU.add, ALU.add,
                accum_out=mv_h[:, j:j + 1]),
                 ins=[M_t], outs=[M_t, mv_t])
            S.op("dve", lambda e, s=s, j=j: e.tensor_tensor(
                mv_h[:, 8 + j:9 + j], ps_a[s][:, 1024:1025],
                invqc_h[:, j:j + 1], ALU.mult),
                 ins=list(ps_a_t[s]) + [invqc_t], outs=[mv_t])
            S.op("dve", lambda e, j=j: e.tensor_scalar(
                mv_h[:, 24 + j:25 + j], mv_h[:, 8 + j:9 + j], EPS_ATT,
                mv_h[:, j:j + 1], ALU.mult, ALU.add),
                 ins=[mv_t], outs=[mv_t])
            S.op("dve", lambda e, j=j: e.reciprocal(
                mv_h[:, 32 + j:33 + j], mv_h[:, 24 + j:25 + j]),
                 ins=[mv_t], outs=[mv_t])
            S.op("dve", lambda e, j=j: e.tensor_scalar(
                mv_h[:, 16 + j:17 + j], mv_h[:, 32 + j:33 + j],
                gam_h[:, :], None, ALU.mult),
                 ins=[mv_t, gam_t], outs=[mv_t])
            S.op("dve", lambda e, msl=msl, j=j: e.tensor_scalar(
                msl, msl, mv_h[:, 16 + j:17 + j], None, ALU.mult),
                 ins=[M_t, mv_t], outs=[M_t])
        for k in range(8):
            c0 = 3072 if k % 2 == 0 else 0
            ps_tp = psv(c0, c0 + 1024)
            ps_tp_t = pbt(c0, c0 + 1024)
            blocks = [(j * 128, M_h[:, j * 1024 + k * 128:
                                    j * 1024 + k * 128 + 128])
                      for j in range(8)]
            transpose_group(ps_tp, blocks, [M_t], ps_tp_t)
            evac_copy(EVAC[k % 2], MT_h[:, k * 1024:(k + 1) * 1024],
                      ps_tp[:, :], list(ps_tp_t), [MT_t])

        ps_m = (psv(0, 1024), psv(1024, 2048))
        ps_m_t = (pbt(0, 1024), pbt(1024, 2048))
        ps_y = psv(2048, 3072)
        ps_y_t = pbt(2048, 3072)
        ps_z = psv(3072, 4096)
        ps_z_t = pbt(3072, 4096)
        dma(xr_h[0][:, :], x_ch[b, 0], [], [xr_t[0]], "D_xr0")
        dma(xr_h[1][:, :], x_ch[b, 1], [], [xr_t[1]], "D_xr1")

        smQ = {(p_, l_): Tr(f"sm{p_}{l_}") for p_ in range(2)
               for l_ in range(2)}

        def ln_chain(i, rb_h, rb_t, ob_h, ob_t, ln):
            s_ = i % 2
            base = s_ * 16 + ln * 8
            smt = smQ[(s_, ln)]
            co = lambda c: sm_h[:, base + c:base + c + 1]
            S.op("act", lambda e: e.activation(co(1), co(0), AF.Copy,
                                               scale=-1.0 / D),
                 ins=[smt], outs=[smt])
            S.op("act", lambda e: e.activation(
                sqln_h[:, :], rb_h[:, :], AF.Square, bias=co(1),
                accum_out=co(2)),
                 ins=[rb_t, smt], outs=[sqln_t, smt])
            S.op("act", lambda e: e.activation(co(3), co(2), AF.Ln,
                                               scale=1.0 / D, bias=eps_h[:, :]),
                 ins=[smt, eps_t], outs=[smt])
            S.op("act", lambda e: e.activation(co(4), co(3), AF.Exp,
                                               scale=-0.5),
                 ins=[smt], outs=[smt])
            S.op("pool", lambda e: e.tensor_tensor(co(5), co(1), co(4),
                                                   ALU.mult),
                 ins=[smt], outs=[smt])
            S.op("pool", lambda e: e.tensor_scalar(
                ob_h[:, :], rb_h[:, :], co(4), co(5), ALU.mult, ALU.add),
                 ins=[rb_t, smt], outs=[ob_t])

        for i in range(NCC + 2):
            s = i % 2
            if 2 <= i <= NCC + 1:
                iy = i - 2
                blocks = [(j * 128, ybf_h[iy % 3][:, j * 128:(j + 1) * 128])
                          for j in range(8)]
                transpose_group(ps_y, blocks, [ybf_t[iy % 3]], ps_y_t)
                evac_copy("dve", yF_h[iy % 2][:, :], ps_y[:, :],
                          list(ps_y_t), [yF_t[iy % 2]])
            if i < NCC:
                def mixfn(e, ps=ps_m[s], i=i):
                    last = None
                    for kk in range(8):
                        lhs = VF_h[:, kk * 2048 + i * 128:
                                   kk * 2048 + i * 128 + 128]
                        for h in range(2):
                            last = e.matmul(
                                ps[:, h * 512:(h + 1) * 512], lhs,
                                MT_h[:, kk * 1024 + h * 512:
                                     kk * 1024 + (h + 1) * 512],
                                start=(kk == 0), stop=(kk == 7))
                    return last
                S.op("pe", mixfn, ins=[VF_t, MT_t], outs=list(ps_m_t[s]))
            if i >= 2:
                i2 = i - 2
                p3 = i2 % 3
                yf = i2 % 2

                def ffnfn(e, yf=yf):
                    last = None
                    for kk in range(8):
                        lhs = yF_h[yf][:, kk * 128:(kk + 1) * 128]
                        for h in range(2):
                            last = e.matmul(
                                ps_z[:, h * 512:(h + 1) * 512], lhs,
                                wtsb_h[:, kk * 1024 + h * 512:
                                       kk * 1024 + (h + 1) * 512],
                                start=(kk == 0), stop=False)
                    for h in range(2):
                        last = e.matmul(
                            ps_z[:, h * 512:(h + 1) * 512],
                            ones_h[0:1, 0:128],
                            brow_h["bfn"][0:1, h * 512:(h + 1) * 512],
                            start=False, stop=True)
                    return last
                S.op("pe", ffnfn, ins=[yF_t[yf], wtsb_t, brow_t["bfn"],
                                       ones_t], outs=list(ps_z_t))
            if i < NCC:
                r3 = i % 3
                S.op("dve", lambda e, s=s, i=i, r3=r3: e.scalar_tensor_tensor(
                    res_h[r3][:, :], ps_m[s][:, :], 1.0, xr_h[s][:, :],
                    ALU.mult, ALU.add,
                    accum_out=sm_h[:, s * 16:s * 16 + 1]),
                     ins=list(ps_m_t[s]) + [xr_t[s]],
                     outs=[res_t[r3], smQ[(s, 0)]])
                if i + 2 < NCC:
                    dma(xr_h[s][:, :], x_ch[b, i + 2], [], [xr_t[s]],
                        f"D_xr{s}")
                ln_chain(i, res_h[r3], res_t[r3], ybf_h[r3], ybf_t[r3], 0)
            if i == 4 and b + 1 < n_batches:
                for q in (0, 1):
                    for k in range(8):
                        dma(xq3[q][:, k, :],
                            xt4[b + 1, k][:, q * 512:(q + 1) * 512],
                            [], [xFq_t[q]], f"D_xq{q}", eng="pool")
            if i >= 2:
                i2 = i - 2
                p3 = i2 % 3
                S.op("dve", lambda e, p3=p3, i2=i2: e.scalar_tensor_tensor(
                    res_h[p3][:, :], ps_z[:, :], 1.0, ybf_h[p3][:, :],
                    ALU.mult, ALU.add,
                    accum_out=sm_h[:, (i2 % 2) * 16 + 8:(i2 % 2) * 16 + 9]),
                     ins=list(ps_z_t) + [ybf_t[p3]],
                     outs=[res_t[p3], smQ[(i2 % 2, 1)]])
                ln_chain(i2, res_h[p3], res_t[p3], ot_h[i2 % 2],
                         ot_t[i2 % 2], 1)
                dma(o_ch[b, i2], ot_h[i2 % 2][:, :], [ot_t[i2 % 2]], [],
                    f"D_out{i2 % 2}", eng="sync")
        if b + 1 < n_batches:
            for q in (2, 3):
                for k in range(8):
                    dma(xq3[q][:, k, :],
                        xt4[b + 1, k][:, q * 512:(q + 1) * 512],
                        [], [xFq_t[q]], f"D_xq{q}", eng="pool")
        S.barrier()

    tot0 = S.counts.get("D_out0", 0)
    tot1 = S.counts.get("D_out1", 0)
    S.raw("sync", lambda e, sems: (e.wait_ge(sems["D_out0"], tot0),
                                   e.wait_ge(sems["D_out1"], tot1)))

    import contextlib
    with contextlib.ExitStack() as stack:
        sems = {}
        for name in S.counts:
            sems[name] = stack.enter_context(nc.semaphore("sem_" + name))
        block = stack.enter_context(nc.Block())

        def replay(key):
            def body(eng):
                for waits, fn, sem, val in S.ops[key]:
                    for sname, v in waits:
                        eng.wait_ge(sems[sname], v)
                    if sem is None:
                        try:
                            fn(eng, sems)
                        except TypeError:
                            fn(eng)
                    else:
                        inst = fn(eng)
                        inst.then_inc(sems[sem],
                                      16 if sem.startswith("D_") else 1)
            return body

        block.tensor(replay("pe"))
        block.vector(replay("dve"))
        block.scalar(replay("act"))
        block.gpsimd(replay("pool"))
        block.sync(replay("sync"))
    return nc


def _prep(inputs):
    f32 = np.float32

    def a(v):
        return np.ascontiguousarray(np.asarray(v, dtype=f32))
    x = a(inputs["L0_attn_x"])
    xt = np.ascontiguousarray(np.transpose(x, (0, 2, 1)))
    common = {
        "wq": a(inputs["L0_attn_Wq"]), "bq": a(inputs["L0_attn_bq"]),
        "wk": a(inputs["L0_attn_Wk"]), "bk": a(inputs["L0_attn_bk"]),
        "wv": a(inputs["L0_attn_Wv"]), "bv": a(inputs["L0_attn_bv"]),
        "gamma": a(inputs["L0_attn_gamma"]),
        "wf": a(inputs["L0_ffn_weight"]), "alpha": a(inputs["L0_ffn_alpha"]),
        "bfn": a(inputs["L0_ffn_bias"]),
    }
    return x, xt, common


def kernel(**inputs):
    x, xt, common = _prep(inputs)
    nc = build(BPC)
    in_maps = [{"x": x[c * BPC:(c + 1) * BPC],
                "xt": xt[c * BPC:(c + 1) * BPC], **common}
               for c in range(NCORE)]
    res = run_bass_kernel_spmd(nc, in_maps, core_ids=list(range(NCORE)))
    return np.concatenate([np.asarray(res.results[c]["out"])
                           for c in range(NCORE)], axis=0)


# version 19
# speedup vs baseline: 1.0696x; 1.0696x over previous
import numpy as np
import concourse.bass as bass
import concourse.mybir as mybir
from concourse.bass_utils import run_bass_kernel_spmd

F32 = mybir.dt.float32
BF16 = mybir.dt.bfloat16
U8 = mybir.dt.uint8
AF = mybir.ActivationFunctionType
ALU = mybir.AluOpType
AX = mybir.AxisListType.X

B_TOT, C, D = 32, 2048, 1024
NCORE = 8
BPC = B_TOT // NCORE
NCC = C // 128
EPS_ATT = 1e-6
LN_EPS = 1e-5
THRESH = 0.7

ENGS = ("pe", "dve", "act", "pool", "sync")


class Tr:
    __slots__ = ("name", "w", "readers", "aliases")

    def __init__(self, name):
        self.name = name
        self.w = {}
        self.readers = {}
        self.aliases = []


class Sched:

    def __init__(self):
        self.counts = {k: 0 for k in ENGS}
        self.ops = {k: [] for k in ENGS}
        self.wm = {k: {} for k in ENGS}
        self.pending = {k: None for k in ENGS}

    def barrier(self):
        snap = {s: self.counts[s] for s in ENGS}
        for e in ENGS:
            self.pending[e] = dict(snap)

    def op(self, eng, fn, ins=(), outs=(), dma=None):
        waits = {}

        def add(s, v):
            if v <= 0 or v <= self.wm[eng].get(s, 0):
                return
            if waits.get(s, 0) < v:
                waits[s] = v

        if self.pending[eng] is not None:
            for s, v in self.pending[eng].items():
                if s != eng:
                    add(s, v)
            self.pending[eng] = None
        allouts = []
        for t in outs:
            allouts.append(t)
            allouts.extend(t.aliases)
        for t in ins:
            for s, v in t.w.items():
                add(s, v)
        for t in allouts:
            for s, v in t.w.items():
                add(s, v)
            for s, v in t.readers.items():
                add(s, v)
        sem = dma if dma else eng
        self.counts[sem] = self.counts.get(sem, 0) + (16 if dma else 1)
        val = self.counts[sem]
        wl = sorted(waits.items())
        for s, v in wl:
            self.wm[eng][s] = max(self.wm[eng].get(s, 0), v)
        self.ops[eng].append((wl, fn, sem, val))
        for t in outs:
            t.w[sem] = val
        for t in ins:
            t.readers[sem] = val
        return (sem, val)

    def raw(self, eng, fn):
        self.ops[eng].append(([], fn, None, None))


def build(n_batches=BPC):
    nc = bass.Bass("TRN2", target_bir_lowering=False)
    S = Sched()

    xdr = nc.dram_tensor("x", [n_batches, C, D], F32, kind="ExternalInput").ap()
    odr = nc.dram_tensor("out", [n_batches, C, D], F32, kind="ExternalOutput").ap()
    wdr = {k: nc.dram_tensor(k, [D, D], F32, kind="ExternalInput").ap()
           for k in ("wq", "wk", "wv", "wf")}
    bdr = {k: nc.dram_tensor(k, [D], F32, kind="ExternalInput").ap()
           for k in ("bq", "bk", "bv", "bfn")}
    gdr = nc.dram_tensor("gamma", [1], F32, kind="ExternalInput").ap()
    adr = nc.dram_tensor("alpha", [1], F32, kind="ExternalInput").ap()
    wt_dram = nc.dram_tensor("wt_dram", [128, 8192], BF16, kind="Internal").ap()
    wtd_t = Tr("wt_dram")

    x_ch = xdr.rearrange("b (n p) d -> b n p d", p=128)
    o_ch = odr.rearrange("b (n p) d -> b n p d", p=128)
    w4 = {k: wdr[k].rearrange("(r p) d -> r p d", p=128)
          for k in ("wq", "wk", "wv", "wf")}

    ARENA = 212_480
    base_h = nc.alloc_sbuf_tensor("arena", [128, ARENA], U8)
    base_addr = nc.lookup_mloc(base_h).addr
    placed = []

    def place(name, shape, dtype, off):
        nbytes = int(np.prod(shape[1:])) * mybir.dt.size(dtype)
        assert off + nbytes <= ARENA, (name, off, nbytes)
        assert off % 32 == 0, name
        h = nc.virtual_sbuf_tensor(name, list(shape), dtype)
        nc._tensor(h.name, list(h.shape), h.dtype, type="SB")
        mloc = nc.lookup_mloc(h)
        mloc.allocated = True
        mloc.addr = base_addr + off
        t = Tr(name)
        placed.append((off, off + nbytes, t))
        return h, t

    WT_h, WT_t = {}, {}
    for i, k in enumerate(("wq", "wk", "wv")):
        WT_h[k], WT_t[k] = place("T_" + k, [128, 8192], BF16, i * 16_384)
    ident_h, ident_t = place("ident", [128, 128], BF16, 49_152)
    ones_h, ones_t = place("ones", [128, 512], BF16, 49_408)
    onesf_h, onesf_t = place("onesf", [128, 128], F32, 50_432)
    brow_h, brow_t = {}, {}
    for i, k in enumerate(("bq", "bk", "bv", "bfn")):
        brow_h[k], brow_t[k] = place("row_" + k, [1, 1024], BF16, 50_944 + i * 2_048)
    invqr_h, invqr_t = place("invq_row", [1, 1024], BF16, 59_136)
    invkr_h, invkr_t = place("invk_row", [1, 1024], BF16, 61_184)
    invqc_h, invqc_t = place("invq_col", [128, 8], F32, 63_232)
    gam_h, gam_t = place("gamma_col", [128, 1], F32, 63_264)
    alp_h, alp_t = place("alpha_col", [128, 1], F32, 63_296)
    sm_h, sm_t = place("smalls", [128, 64], F32, 63_328)
    mv_h, mv_t = place("mvecs", [128, 40], F32, 63_584)
    bvc_h, bvc_t = place("bv_col", [128, 8], F32, 63_776)
    eps_h, eps_t = place("eps_col", [128, 1], F32, 63_808)

    PS_h = nc.alloc_psum_tensor("PS", [128, 4096], F32)
    bank_tr = [Tr(f"psbank{i}") for i in range(8)]

    def psv(c0, c1):
        return PS_h[:, c0:c1]

    def pbt(c0, c1):
        return bank_tr[c0 // 512:(c1 + 511) // 512]

    xF_h, xF_t = place("x_F", [128, 16384], BF16, 65_024)
    xFc_t = [Tr(f"xFc{i}") for i in range(NCC)]
    M_h, M_t = place("M_all", [128, 8192], BF16, 65_024)
    wtsb_h, wtsb_t = place("wt_sb", [128, 8192], BF16, 81_408)
    wnatc_h, wnatc_t = zip(*[place(f"wnat{r}", [128, 1024], F32,
                                   97_792 + r * 4_096) for r in range(8)])
    QT_h, QT_t = place("Q_T", [128, 16384], BF16, 97_792)
    xr_h, xr_t = zip(*[place(f"xr{s}", [128, 1024], F32, 97_792 + s * 4_096)
                       for s in range(2)])
    ot_h, ot_t = zip(*[place(f"out_t{s}", [128, 1024], F32, 105_984 + s * 4_096)
                       for s in range(2)])
    res_h, res_t = zip(*[place(f"res{s}", [128, 1024], F32, 114_176 + s * 4_096)
                         for s in range(2)])
    res3_h, res3_t = place("res2_", [128, 1024], F32, 155_136)
    res_h = list(res_h) + [res3_h]
    res_t = list(res_t) + [res3_t]
    sqq_h, sqq_t = place("sq_q", [128, 1024], BF16, 114_176)
    sqk_h, sqk_t = place("sq_k", [128, 1024], BF16, 116_224)
    invkb_h, invkb_t = place("invk_bcast", [128, 1024], BF16, 116_224)
    ybf_h, ybf_t = zip(*[place(f"ybf{s}", [128, 1024], BF16, 122_368 + s * 2_048)
                         for s in range(3)])
    wnbf_h, wnbf_t = place("wnbf", [128, 8192], BF16, 130_560)
    KT_h, KT_t = place("K_T", [128, 16384], BF16, 130_560)
    MT_h, MT_t = place("MT_all", [128, 8192], BF16, 130_560)
    yF_h, yF_t = zip(*[place(f"yF{s}", [128, 1024], BF16, 146_944 + s * 2_048)
                       for s in range(2)])
    sqln_h, sqln_t = place("sq_ln", [128, 1024], F32, 151_040)
    tmpa_h, tmpa_t = place("tmpa", [128, 1024], F32, 146_944)
    tmpb_h, tmpb_t = place("tmpb", [128, 1024], F32, 151_040)
    VF_h, VF_t = place("V_F", [128, 16384], BF16, 163_328)
    wfnc_h, wfnc_t = zip(*[place(f"wfn{r}", [128, 1024], F32,
                                 163_328 + r * 4_096) for r in range(8)])
    ternc_h, ternc_t = zip(*[place(f"tern{t}", [128, 1024], BF16,
                                   163_328 + t * 2_048) for t in range(8)])
    ternW_h, ternW_t = place("ternW", [128, 8192], BF16, 163_328)
    wnbfv_h, wnbfv_t = place("wnbf_v", [128, 8192], BF16, 179_712)
    wst_h, wst_t = place("wt_stage", [128, 8192], BF16, 179_712)
    xin_h, xin_t = zip(*[place(f"xin{s}", [128, 1024], F32, 196_096 + s * 4_096)
                         for s in range(2)])
    xbf_h, xbf_t = zip(*[place(f"xbf{s}", [128, 1024], BF16, 204_288 + s * 2_048)
                         for s in range(2)])
    bqb_h, bqb_t = place("bq_bcast", [128, 1024], BF16, 208_384)
    bkb_h, bkb_t = place("bk_bcast", [128, 1024], BF16, 210_432)
    s1_h, s1_t = zip(*[place(f"s1_{s}", [128, 1024], BF16, 196_096 + s * 2_048)
                       for s in range(2)])

    for i, (lo, hi, t) in enumerate(placed):
        for j, (lo2, hi2, t2) in enumerate(placed):
            if i != j and lo < hi2 and lo2 < hi:
                t.aliases.append(t2)

    xF3 = xF_h[:, :].rearrange("p (k c) -> p k c", c=2048)

    def dma(dst_ap, src_ap, ins, outs, sem, eng="sync"):
        S.op(eng, lambda e, d=dst_ap, s=src_ap: e.dma_start(out=d, in_=s),
             ins=list(ins), outs=list(outs), dma=sem)

    def transpose_group(psum_ap, blocks, ins, ps_trs):
        def fn(e, ps=psum_ap, bl=tuple(blocks)):
            last = None
            for out_off, lhs in bl:
                last = e.matmul(ps[:, out_off:out_off + 128], lhs,
                                ident_h[:, :], start=True, stop=True)
            return last
        S.op("pe", fn, ins=list(ins) + [ident_t], outs=list(ps_trs))

    EVAC = ("dve", "act")

    def evac_copy(eng, dst_ap, src_ap, ins, outs):
        if eng == "act":
            S.op("act", lambda e, d=dst_ap, s=src_ap: e.activation(
                d, s, AF.Copy), ins=list(ins), outs=list(outs))
        else:
            S.op("dve", lambda e, d=dst_ap, s=src_ap: e.tensor_copy(d, s),
                 ins=list(ins), outs=list(outs))

    def xcast(i):
        s = i % 2
        S.op("pool", lambda e, s=s: e.tensor_copy(xbf_h[s][:, :],
                                                  xin_h[s][:, :]),
             ins=[xin_t[s]], outs=[xbf_t[s]])

    def xtrans(i):
        s = i % 2
        c0 = s * 1024
        ps = psv(c0, c0 + 1024)
        blocks = [(j * 128, xbf_h[s][:, j * 128:(j + 1) * 128])
                  for j in range(8)]
        transpose_group(ps, blocks, [xbf_t[s]], pbt(c0, c0 + 1024))
        dst = xF3[:, :, i * 128:(i + 1) * 128]
        evac_copy("act", dst, ps.rearrange("p (k c) -> p k c", c=128),
                  pbt(c0, c0 + 1024), [xF_t, xFc_t[i]])

    S.op("pool", lambda e: e.memset(ones_h[:, :], 1.0), outs=[ones_t])
    S.op("pool", lambda e: e.memset(eps_h[:, :], LN_EPS), outs=[eps_t])
    S.op("pool", lambda e: e.memset(onesf_h[:, :], 1.0), outs=[onesf_t])
    S.op("pool", lambda e: e.iota(tmpa_h[:, 0:128], [[1, 128]],
                                  channel_multiplier=0,
                                  allow_small_or_imprecise_dtypes=True),
         outs=[tmpa_t])
    S.op("pool", lambda e: e.iota(tmpb_h[:, 0:128], [[0, 128]],
                                  channel_multiplier=1,
                                  allow_small_or_imprecise_dtypes=True),
         outs=[tmpb_t])
    S.op("dve", lambda e: e.tensor_tensor(ident_h[:, :], tmpa_h[:, 0:128],
                                          tmpb_h[:, 0:128], ALU.is_equal),
         ins=[tmpa_t, tmpb_t], outs=[ident_t])
    for k in ("bq", "bk", "bv", "bfn"):
        dma(brow_h[k][0:1, :], bdr[k][None, :], [], [brow_t[k]],
            "D_b" + k, eng="pool")
    g_sb = sm_h[0:1, 32:33]
    a_sb = sm_h[0:1, 33:34]
    dma(g_sb, gdr[None, :], [], [sm_t], "D_sm")
    dma(a_sb, adr[None, :], [], [sm_t], "D_sm")
    dma(xin_h[0][:, :], x_ch[0, 0], [], [xin_t[0]], "D_xin0")
    dma(xin_h[1][:, :], x_ch[0, 1], [], [xin_t[1]], "D_xin1")
    xcast(0)
    xcast(1)
    dma(xin_h[0][:, :], x_ch[0, 2], [], [xin_t[0]], "D_xin0")
    for r in range(8):
        dma(wfnc_h[r][:, :], w4["wf"][r], [], [wfnc_t[r]], f"D_wf{r}",
            eng="act")
    for r in range(8):
        S.op("act", lambda e, r=r: e.activation(
            tmpa_h[:, :], wfnc_h[r][:, :], AF.Abs,
            accum_out=sm_h[:, 48 + r:49 + r]),
             ins=[wfnc_t[r]], outs=[tmpa_t, sm_t])

    ps_b = psv(2048, 2560)
    ps_b2 = psv(2560, 3072)

    def bvcols(e):
        last = None
        for jj in range(8):
            last = e.matmul(ps_b[:, jj:jj + 1],
                            brow_h["bv"][0:1, jj * 128:(jj + 1) * 128],
                            ones_h[0:1, 0:1], start=True, stop=True)
        return last
    S.op("pe", bvcols, ins=[brow_t["bv"], ones_t], outs=pbt(2048, 2560))
    S.op("dve", lambda e: e.tensor_copy(bvc_h[:, :], ps_b[:, 0:8]),
         ins=pbt(2048, 2560), outs=[bvc_t])
    S.op("pe", lambda e: e.matmul(ps_b[:, 8:9], onesf_h[0:1, :], g_sb,
                                  start=True, stop=True),
         ins=[onesf_t, sm_t], outs=pbt(2048, 2560))
    S.op("pe", lambda e: e.matmul(ps_b[:, 9:10], onesf_h[0:1, :], a_sb,
                                  start=True, stop=True),
         ins=[onesf_t, sm_t], outs=pbt(2048, 2560))
    S.op("dve", lambda e: e.tensor_copy(gam_h[:, :], ps_b[:, 8:9]),
         ins=pbt(2048, 2560), outs=[gam_t])
    S.op("dve", lambda e: e.tensor_copy(alp_h[:, :], ps_b[:, 9:10]),
         ins=pbt(2048, 2560), outs=[alp_t])
    for bb_h, bb_t, key in ((bqb_h, bqb_t, "bq"), (bkb_h, bkb_t, "bk")):
        for h in range(2):
            S.op("pe", lambda e, h=h, key=key: e.matmul(
                ps_b2[:, 0:512], ones_h[0:1, 0:128],
                brow_h[key][0:1, h * 512:(h + 1) * 512],
                start=True, stop=True),
                 ins=[brow_t[key], ones_t], outs=pbt(2560, 3072))
            S.op("dve", lambda e, h=h, bb_h=bb_h: e.tensor_copy(
                bb_h[:, h * 512:(h + 1) * 512], ps_b2[:, 0:512]),
                 ins=pbt(2560, 3072), outs=[bb_t])

    xtrans(0)
    xtrans(1)
    xcast(2)

    def load_weight(key, dst_h, dst_t):
        for r in range(8):
            dma(wnatc_h[r][:, :], w4[key][r], [], [wnatc_t[r]], f"D_wn{r}")
        for r in range(8):
            S.op("dve", lambda e, r=r, dst_h=dst_h: e.tensor_copy(
                dst_h[:, r * 1024:(r + 1) * 1024], wnatc_h[r][:, :]),
                 ins=[wnatc_t[r]], outs=[dst_t])

    def transpose_weight(src_h, src_ts, out_h, out_t, s8):
        c0 = (s8 % 2) * 1024
        ps_w = psv(c0, c0 + 1024)
        blocks = [(r * 128, src_h[:, r * 1024 + s8 * 128:
                                  r * 1024 + s8 * 128 + 128])
                  for r in range(8)]
        transpose_group(ps_w, blocks, src_ts, pbt(c0, c0 + 1024))
        evac_copy(EVAC[s8 % 2], out_h[:, s8 * 1024:(s8 + 1) * 1024],
                  ps_w[:, :], pbt(c0, c0 + 1024), [out_t])

    load_weight("wq", wnbf_h, wnbf_t)
    for s8 in range(8):
        transpose_weight(wnbf_h, [wnbf_t], WT_h["wq"], WT_t["wq"], s8)
    S.op("dve", lambda e: e.tensor_reduce(sm_h[:, 34:35], sm_h[:, 48:56],
                                          AX, ALU.add),
         ins=[sm_t], outs=[sm_t])
    ps_thr = psv(3072, 3584)
    S.op("pe", lambda e: e.matmul(ps_thr[:, 0:1], onesf_h[:, :],
                                  sm_h[:, 34:35], start=True, stop=True),
         ins=[onesf_t, sm_t], outs=pbt(3072, 3584))
    S.op("dve", lambda e: e.tensor_scalar(sm_h[:, 35:36], ps_thr[:, 0:1],
                                          THRESH / float(D * D), None,
                                          ALU.mult),
         ins=pbt(3072, 3584), outs=[sm_t])
    S.op("dve", lambda e: e.tensor_scalar(sm_h[:, 36:37], sm_h[:, 35:36],
                                          -1.0, None, ALU.mult),
         ins=[sm_t], outs=[sm_t])
    load_weight("wk", wnbf_h, wnbf_t)
    for s8 in range(8):
        transpose_weight(wnbf_h, [wnbf_t], WT_h["wk"], WT_t["wk"], s8)
    for t in range(8):
        S.op("dve", lambda e, t=t: e.tensor_scalar(
            tmpa_h[:, :], wfnc_h[t][:, :], sm_h[:, 35:36], alp_h[:, 0:1],
            ALU.is_gt, ALU.mult),
             ins=[wfnc_t[t], sm_t, alp_t], outs=[tmpa_t])
        S.op("dve", lambda e, t=t: e.tensor_scalar(
            tmpb_h[:, :], wfnc_h[t][:, :], sm_h[:, 36:37], alp_h[:, 0:1],
            ALU.is_lt, ALU.mult),
             ins=[wfnc_t[t], sm_t, alp_t], outs=[tmpb_t])
        S.op("dve", lambda e, t=t: e.tensor_tensor(
            ternc_h[t][:, :], tmpa_h[:, :], tmpb_h[:, :], ALU.subtract),
             ins=[tmpa_t, tmpb_t], outs=[ternc_t[t]])
    load_weight("wv", wnbfv_h, wnbfv_t)

    for b in range(n_batches):
        ps_q, ps_k = psv(2048, 3072), psv(3072, 4096)
        ps_q_t, ps_k_t = pbt(2048, 3072), pbt(3072, 4096)

        def proj_group(ps_ap, ps_tr, wk_, i):
            def fn(e, ps=ps_ap, i=i, wk_=wk_):
                last = None
                for kk in range(8):
                    lhs = xF3[:, kk, i * 128:(i + 1) * 128]
                    for h in range(2):
                        last = e.matmul(
                            ps[:, h * 512:(h + 1) * 512], lhs,
                            WT_h[wk_][:, kk * 1024 + h * 512:
                                       kk * 1024 + (h + 1) * 512],
                            start=(kk == 0), stop=(kk == 7))
                return last
            sv = S.op("pe", fn, ins=[xFc_t[i], WT_t[wk_]], outs=list(ps_tr))
            xF_t.readers[sv[0]] = sv[1]

        def evac_qk(dst_ap, ps_ap, ps_tr, bb_h, bb_t, out_t):
            S.op("dve", lambda e, d=dst_ap, ps=ps_ap, bb=bb_h:
                 e.scalar_tensor_tensor(d, ps[:, :], 1.0, bb[:, :],
                                        ALU.mult, ALU.add),
                 ins=list(ps_tr) + [bb_t], outs=[out_t])

        extra_pe = {}
        if b == 0:
            for s8 in range(8):
                extra_pe.setdefault(3 + s8 // 2, []).append(
                    lambda s8=s8: transpose_weight(
                        wnbfv_h, [wnbfv_t], WT_h["wv"], WT_t["wv"], s8))
            for s8 in range(8):
                extra_pe.setdefault(7 + s8 // 2, []).append(
                    lambda s8=s8: transpose_weight(
                        ternW_h, list(ternc_t), wst_h, wst_t, s8))
            extra_pe.setdefault(11, []).append(
                lambda: dma(wt_dram, wst_h[:, :], [wst_t], [wtd_t], "D_wtd"))

        if b > 0:
            xtrans(0)
            xtrans(1)
            xcast(2)
        for i in range(NCC):
            if i + 3 < NCC:
                dma(xin_h[(i + 3) % 2][:, :], x_ch[b, i + 3], [],
                    [xin_t[(i + 3) % 2]], f"D_xin{(i + 3) % 2}")
            proj_group(ps_q, ps_q_t, "wq", i)
            evac_qk(QT_h[:, i * 1024:(i + 1) * 1024], ps_q, ps_q_t,
                    bqb_h, bqb_t, QT_t)
            proj_group(ps_k, ps_k_t, "wk", i)
            evac_qk(KT_h[:, i * 1024:(i + 1) * 1024], ps_k, ps_k_t,
                    bkb_h, bkb_t, KT_t)
            if i + 2 < NCC:
                xtrans(i + 2)
            if i + 3 < NCC:
                xcast(i + 3)
            for fn in extra_pe.get(i, []):
                fn()

        ps_v = (psv(0, 1024), psv(1024, 2048))
        ps_v_t = (pbt(0, 1024), pbt(1024, 2048))
        ps_cq, ps_ck = psv(2048, 3072), psv(3072, 4096)
        ps_cq_t, ps_ck_t = pbt(2048, 3072), pbt(3072, 4096)

        def colsum_group(ps_ap, ps_tr, sqh, sqtr, i):
            def fn(e, ps=ps_ap, sqh=sqh, i=i):
                last = None
                for h in range(2):
                    last = e.matmul(ps[0:1, h * 512:(h + 1) * 512],
                                    ones_h[:, 0:1],
                                    sqh[:, h * 512:(h + 1) * 512],
                                    start=(i == 0), stop=(i == NCC - 1))
                return last
            S.op("pe", fn, ins=[sqtr, ones_t], outs=list(ps_tr))

        for it in range(16):
            l, half = it // 2, it % 2
            s = it % 2

            def vfn(e, ps=ps_v[s], l=l, half=half):
                last = None
                for kk in range(8):
                    lhs = WT_h["wv"][:, kk * 1024 + l * 128:
                                     kk * 1024 + l * 128 + 128]
                    for q2 in range(2):
                        c0 = half * 1024 + q2 * 512
                        last = e.matmul(ps[:, q2 * 512:(q2 + 1) * 512],
                                        lhs, xF3[:, kk, c0:c0 + 512],
                                        start=(kk == 0), stop=(kk == 7))
                return last
            S.op("pe", vfn, ins=[xF_t, WT_t["wv"]], outs=list(ps_v_t[s]))
            dst = VF_h[:, l * 2048 + half * 1024:l * 2048 + half * 1024 + 1024]
            S.op("dve", lambda e, d=dst, ps=ps_v[s], l=l: e.tensor_scalar(
                d, ps[:, :], bvc_h[:, l:l + 1], None, ALU.add),
                 ins=list(ps_v_t[s]) + [bvc_t], outs=[VF_t])
            qsl = QT_h[:, it * 1024:(it + 1) * 1024]
            ksl = KT_h[:, it * 1024:(it + 1) * 1024]
            S.op("pool", lambda e, a=qsl: e.tensor_tensor(
                sqq_h[:, :], a, a, ALU.mult), ins=[QT_t], outs=[sqq_t])
            colsum_group(ps_cq, ps_cq_t, sqq_h, sqq_t, it)
            S.op("pool", lambda e, a=ksl: e.tensor_tensor(
                sqk_h[:, :], a, a, ALU.mult), ins=[KT_t], outs=[sqk_t])
            colsum_group(ps_ck, ps_ck_t, sqk_h, sqk_t, it)

        stage = s1_h[0]
        S.op("act", lambda e: e.activation(stage[0:1, 0:1024],
                                           ps_cq[0:1, 0:1024], AF.Ln),
             ins=list(ps_cq_t), outs=[s1_t[0]])
        S.op("act", lambda e: e.activation(invqr_h[0:1, :],
                                           stage[0:1, 0:1024], AF.Exp,
                                           scale=-0.5),
             ins=[s1_t[0]], outs=[invqr_t])

        S.op("act", lambda e: e.activation(stage[0:1, 0:1024],
                                           ps_ck[0:1, 0:1024], AF.Ln),
             ins=list(ps_ck_t), outs=[s1_t[0]])

        S.op("act", lambda e: e.activation(invkr_h[0:1, :],
                                           stage[0:1, 0:1024], AF.Exp,
                                           scale=-0.5),
             ins=[s1_t[0]], outs=[invkr_t])

        dma(wtsb_h[:, :], wt_dram, [wtd_t], [wtsb_t], "D_wtsb")

        ps_a = (psv(0, 1536), psv(1536, 3072))
        ps_a_t = (pbt(0, 1536), pbt(1536, 3072))
        for j in range(8):
            s = j % 2

            def afn(e, ps=ps_a[s], j=j):
                last = None
                for i in range(NCC):
                    lhs = QT_h[:, i * 1024 + j * 128:i * 1024 + j * 128 + 128]
                    st, sp = (i == 0), (i == NCC - 1)
                    last = e.matmul(ps[:, 0:512], lhs,
                                    KT_h[:, i * 1024:i * 1024 + 512],
                                    start=st, stop=sp)
                    last = e.matmul(ps[:, 512:1024], lhs,
                                    KT_h[:, i * 1024 + 512:i * 1024 + 1024],
                                    start=st, stop=sp)
                    last = e.matmul(ps[:, 1024:1025], lhs, ones_h[:, 0:1],
                                    start=st, stop=sp)
                return last
            S.op("pe", afn, ins=[QT_t, KT_t, ones_t], outs=list(ps_a_t[s]))
            if j == 0:
                def invq_cols(e):
                    last = None
                    for jj in range(8):
                        last = e.matmul(psv(3584, 3592)[:, jj:jj + 1],
                                        invqr_h[0:1, jj * 128:(jj + 1) * 128],
                                        ones_h[0:1, 0:1],
                                        start=True, stop=True)
                    return last
                S.op("pe", invq_cols, ins=[invqr_t, ones_t],
                     outs=pbt(3584, 3592))
                S.op("dve", lambda e: e.tensor_copy(invqc_h[:, :],
                                                    psv(3584, 3592)[:, 0:8]),
                     ins=pbt(3584, 3592), outs=[invqc_t])
                for h in range(2):
                    S.op("pe", lambda e, h=h: e.matmul(
                        psv(3072, 3584)[:, 0:512], ones_h[0:1, 0:128],
                        invkr_h[0:1, h * 512:(h + 1) * 512],
                        start=True, stop=True),
                         ins=[invkr_t, ones_t], outs=pbt(3072, 3584))
                    S.op("dve", lambda e, h=h: e.tensor_copy(
                        invkb_h[:, h * 512:(h + 1) * 512],
                        psv(3072, 3584)[:, 0:512]),
                         ins=pbt(3072, 3584), outs=[invkb_t])
            S.op("act", lambda e, s=s, j=j: e.activation(
                s1_h[s][:, :], ps_a[s][:, 0:1024], AF.Copy,
                bias=0.0, scale=invqc_h[:, j:j + 1]),
                 ins=list(ps_a_t[s]) + [invqc_t], outs=[s1_t[s]])
            msl = M_h[:, j * 1024:(j + 1) * 1024]
            S.op("dve", lambda e, s=s, msl=msl: e.tensor_tensor(
                msl, s1_h[s][:, :], invkb_h[:, :], ALU.mult),
                 ins=[s1_t[s], invkb_t], outs=[M_t])
            S.op("dve", lambda e, msl=msl, j=j: e.tensor_scalar(
                msl, msl, 1.0, 0.0, ALU.add, ALU.add,
                accum_out=mv_h[:, j:j + 1]),
                 ins=[M_t], outs=[M_t, mv_t])
            S.op("dve", lambda e, s=s, j=j: e.tensor_tensor(
                mv_h[:, 8 + j:9 + j], ps_a[s][:, 1024:1025],
                invqc_h[:, j:j + 1], ALU.mult),
                 ins=list(ps_a_t[s]) + [invqc_t], outs=[mv_t])
            S.op("dve", lambda e, j=j: e.tensor_scalar(
                mv_h[:, 24 + j:25 + j], mv_h[:, 8 + j:9 + j], EPS_ATT,
                mv_h[:, j:j + 1], ALU.mult, ALU.add),
                 ins=[mv_t], outs=[mv_t])
            S.op("dve", lambda e, j=j: e.reciprocal(
                mv_h[:, 32 + j:33 + j], mv_h[:, 24 + j:25 + j]),
                 ins=[mv_t], outs=[mv_t])
            S.op("dve", lambda e, j=j: e.tensor_scalar(
                mv_h[:, 16 + j:17 + j], mv_h[:, 32 + j:33 + j],
                gam_h[:, :], None, ALU.mult),
                 ins=[mv_t, gam_t], outs=[mv_t])
            S.op("dve", lambda e, msl=msl, j=j: e.tensor_scalar(
                msl, msl, mv_h[:, 16 + j:17 + j], None, ALU.mult),
                 ins=[M_t, mv_t], outs=[M_t])
        for k in range(8):
            c0 = 3072 if k % 2 == 0 else 0
            ps_tp = psv(c0, c0 + 1024)
            ps_tp_t = pbt(c0, c0 + 1024)
            blocks = [(j * 128, M_h[:, j * 1024 + k * 128:
                                    j * 1024 + k * 128 + 128])
                      for j in range(8)]
            transpose_group(ps_tp, blocks, [M_t], ps_tp_t)
            evac_copy(EVAC[k % 2], MT_h[:, k * 1024:(k + 1) * 1024],
                      ps_tp[:, :], list(ps_tp_t), [MT_t])

        ps_m = (psv(0, 1024), psv(1024, 2048))
        ps_m_t = (pbt(0, 1024), pbt(1024, 2048))
        ps_y = psv(2048, 3072)
        ps_y_t = pbt(2048, 3072)
        ps_z = psv(3072, 4096)
        ps_z_t = pbt(3072, 4096)
        dma(xr_h[0][:, :], x_ch[b, 0], [], [xr_t[0]], "D_xr0")
        dma(xr_h[1][:, :], x_ch[b, 1], [], [xr_t[1]], "D_xr1")

        smQ = {(p_, l_): Tr(f"sm{p_}{l_}") for p_ in range(2)
               for l_ in range(2)}

        def ln_chain(i, rb_h, rb_t, ob_h, ob_t, ln):
            s_ = i % 2
            base = s_ * 16 + ln * 8
            smt = smQ[(s_, ln)]
            co = lambda c: sm_h[:, base + c:base + c + 1]
            S.op("act", lambda e: e.activation(co(1), co(0), AF.Copy,
                                               scale=-1.0 / D),
                 ins=[smt], outs=[smt])
            S.op("act", lambda e: e.activation(
                sqln_h[:, :], rb_h[:, :], AF.Square, bias=co(1),
                accum_out=co(2)),
                 ins=[rb_t, smt], outs=[sqln_t, smt])
            S.op("act", lambda e: e.activation(co(3), co(2), AF.Ln,
                                               scale=1.0 / D, bias=eps_h[:, :]),
                 ins=[smt, eps_t], outs=[smt])
            S.op("act", lambda e: e.activation(co(4), co(3), AF.Exp,
                                               scale=-0.5),
                 ins=[smt], outs=[smt])
            S.op("pool", lambda e: e.tensor_tensor(co(5), co(1), co(4),
                                                   ALU.mult),
                 ins=[smt], outs=[smt])
            S.op("pool", lambda e: e.tensor_scalar(
                ob_h[:, :], rb_h[:, :], co(4), co(5), ALU.mult, ALU.add),
                 ins=[rb_t, smt], outs=[ob_t])

        for i in range(NCC + 2):
            s = i % 2
            if 2 <= i <= NCC + 1:
                iy = i - 2
                blocks = [(j * 128, ybf_h[iy % 3][:, j * 128:(j + 1) * 128])
                          for j in range(8)]
                transpose_group(ps_y, blocks, [ybf_t[iy % 3]], ps_y_t)
                evac_copy("dve", yF_h[iy % 2][:, :], ps_y[:, :],
                          list(ps_y_t), [yF_t[iy % 2]])
            if i < NCC:
                def mixfn(e, ps=ps_m[s], i=i):
                    last = None
                    for kk in range(8):
                        lhs = VF_h[:, kk * 2048 + i * 128:
                                   kk * 2048 + i * 128 + 128]
                        for h in range(2):
                            last = e.matmul(
                                ps[:, h * 512:(h + 1) * 512], lhs,
                                MT_h[:, kk * 1024 + h * 512:
                                     kk * 1024 + (h + 1) * 512],
                                start=(kk == 0), stop=(kk == 7))
                    return last
                S.op("pe", mixfn, ins=[VF_t, MT_t], outs=list(ps_m_t[s]))
            if i >= 2:
                i2 = i - 2
                p3 = i2 % 3
                yf = i2 % 2

                def ffnfn(e, yf=yf):
                    last = None
                    for kk in range(8):
                        lhs = yF_h[yf][:, kk * 128:(kk + 1) * 128]
                        for h in range(2):
                            last = e.matmul(
                                ps_z[:, h * 512:(h + 1) * 512], lhs,
                                wtsb_h[:, kk * 1024 + h * 512:
                                       kk * 1024 + (h + 1) * 512],
                                start=(kk == 0), stop=False)
                    for h in range(2):
                        last = e.matmul(
                            ps_z[:, h * 512:(h + 1) * 512],
                            ones_h[0:1, 0:128],
                            brow_h["bfn"][0:1, h * 512:(h + 1) * 512],
                            start=False, stop=True)
                    return last
                S.op("pe", ffnfn, ins=[yF_t[yf], wtsb_t, brow_t["bfn"],
                                       ones_t], outs=list(ps_z_t))
            if i < NCC:
                r3 = i % 3
                S.op("dve", lambda e, s=s, i=i, r3=r3: e.scalar_tensor_tensor(
                    res_h[r3][:, :], ps_m[s][:, :], 1.0, xr_h[s][:, :],
                    ALU.mult, ALU.add,
                    accum_out=sm_h[:, s * 16:s * 16 + 1]),
                     ins=list(ps_m_t[s]) + [xr_t[s]],
                     outs=[res_t[r3], smQ[(s, 0)]])
                if i + 2 < NCC:
                    dma(xr_h[s][:, :], x_ch[b, i + 2], [], [xr_t[s]],
                        f"D_xr{s}")
                ln_chain(i, res_h[r3], res_t[r3], ybf_h[r3], ybf_t[r3], 0)
            if i >= 2:
                i2 = i - 2
                p3 = i2 % 3
                S.op("dve", lambda e, p3=p3, i2=i2: e.scalar_tensor_tensor(
                    res_h[p3][:, :], ps_z[:, :], 1.0, ybf_h[p3][:, :],
                    ALU.mult, ALU.add,
                    accum_out=sm_h[:, (i2 % 2) * 16 + 8:(i2 % 2) * 16 + 9]),
                     ins=list(ps_z_t) + [ybf_t[p3]],
                     outs=[res_t[p3], smQ[(i2 % 2, 1)]])
                ln_chain(i2, res_h[p3], res_t[p3], ot_h[i2 % 2],
                         ot_t[i2 % 2], 1)
                dma(o_ch[b, i2], ot_h[i2 % 2][:, :], [ot_t[i2 % 2]], [],
                    f"D_out{i2 % 2}", eng="sync")
                if i2 == 10 and b + 1 < n_batches:
                    dma(xin_h[0][:, :], x_ch[b + 1, 0], [], [xin_t[0]],
                        "D_xin0")
                    dma(xin_h[1][:, :], x_ch[b + 1, 1], [], [xin_t[1]],
                        "D_xin1")
                    S.op("pool", lambda e: e.tensor_copy(xbf_h[0][:, :],
                                                         xin_h[0][:, :]),
                         ins=[xin_t[0]], outs=[xbf_t[0]])
                    S.op("pool", lambda e: e.tensor_copy(xbf_h[1][:, :],
                                                         xin_h[1][:, :]),
                         ins=[xin_t[1]], outs=[xbf_t[1]])
                    dma(xin_h[0][:, :], x_ch[b + 1, 2], [], [xin_t[0]],
                        "D_xin0")
        S.barrier()

    tot0 = S.counts.get("D_out0", 0)
    tot1 = S.counts.get("D_out1", 0)
    S.raw("sync", lambda e, sems: (e.wait_ge(sems["D_out0"], tot0),
                                   e.wait_ge(sems["D_out1"], tot1)))

    import contextlib
    with contextlib.ExitStack() as stack:
        sems = {}
        for name in S.counts:
            sems[name] = stack.enter_context(nc.semaphore("sem_" + name))
        block = stack.enter_context(nc.Block())

        def replay(key):
            def body(eng):
                for waits, fn, sem, val in S.ops[key]:
                    for sname, v in waits:
                        eng.wait_ge(sems[sname], v)
                    if sem is None:
                        try:
                            fn(eng, sems)
                        except TypeError:
                            fn(eng)
                    else:
                        inst = fn(eng)
                        inst.then_inc(sems[sem],
                                      16 if sem.startswith("D_") else 1)
            return body

        block.tensor(replay("pe"))
        block.vector(replay("dve"))
        block.scalar(replay("act"))
        block.gpsimd(replay("pool"))
        block.sync(replay("sync"))
    return nc


def _prep(inputs):
    f32 = np.float32

    def a(v):
        return np.ascontiguousarray(np.asarray(v, dtype=f32))
    x = a(inputs["L0_attn_x"])
    common = {
        "wq": a(inputs["L0_attn_Wq"]), "bq": a(inputs["L0_attn_bq"]),
        "wk": a(inputs["L0_attn_Wk"]), "bk": a(inputs["L0_attn_bk"]),
        "wv": a(inputs["L0_attn_Wv"]), "bv": a(inputs["L0_attn_bv"]),
        "gamma": a(inputs["L0_attn_gamma"]),
        "wf": a(inputs["L0_ffn_weight"]), "alpha": a(inputs["L0_ffn_alpha"]),
        "bfn": a(inputs["L0_ffn_bias"]),
    }
    return x, common


def kernel(**inputs):
    x, common = _prep(inputs)
    nc = build(BPC)
    in_maps = [{"x": x[c * BPC:(c + 1) * BPC], **common} for c in range(NCORE)]
    res = run_bass_kernel_spmd(nc, in_maps, core_ids=list(range(NCORE)))
    return np.concatenate([np.asarray(res.results[c]["out"])
                           for c in range(NCORE)], axis=0)
